# version 26
# baseline (speedup 1.0000x reference)
import numpy as np
import ml_dtypes
from contextlib import ExitStack
from collections import defaultdict

import concourse.bass as bass
import concourse.mybir as mybir
from concourse.bass_utils import run_bass_kernel_spmd

F32 = mybir.dt.float32
BF16 = mybir.dt.bfloat16
AF = mybir.ActivationFunctionType
ALU = mybir.AluOpType

N_CORES = 8
NB = 2
T = 2048
D = 2048
NT = 16
ND = 16
NH = 16
NCH = 4
NHT = 64
EPS = 1e-5
RSQ = float(1.0 / np.sqrt(128.0))


class Sem:
    __slots__ = ("h", "v")

    def __init__(self, h):
        self.h = h
        self.v = 0


def inc(instr, s, n=1):
    instr.then_inc(s.h, n)
    s.v += n
    return s.v


class Ctx:
    pass


def build_nc(nb=NB, upto=None, taps=False):
    V = defaultdict(int)
    for pass_i in range(2):
        nc = bass.Bass("TRN2", target_bir_lowering=False)
        c = Ctx()
        c.nc = nc
        c.V = V
        with ExitStack() as es:
            c.es = es
            _declare(c, nb)
            _emit(c, nb, upto, taps)
    return nc


def _declare(c, nb):
    nc = c.nc
    dr = lambda name, shape, dt, kind: nc.dram_tensor(name, shape, dt, kind=kind).ap()
    c.x_hb = dr("x", [nb, T, D], F32, "ExternalInput")
    c.wq_hb = dr("wq", [NH, 128, 2048], F32, "ExternalInput")
    c.wk_hb = dr("wk", [NH, 128, 2048], F32, "ExternalInput")
    c.wv_hb = dr("wv", [NH, 128, 2048], F32, "ExternalInput")
    c.wo_hb = dr("wo", [4, 128, 8192], F32, "ExternalInput")
    c.bqk_hb = dr("bqk", [128, 32], F32, "ExternalInput")
    c.bv_hb = dr("bv", [1, D], F32, "ExternalInput")
    c.bo_hb = dr("bo", [1, D], F32, "ExternalInput")
    c.wg_hb = dr("wg", [NHT, 128, 2048], F32, "ExternalInput")
    c.wu_hb = dr("wu", [NHT, 128, 2048], F32, "ExternalInput")
    c.wd_hb = dr("wd", [8, 4, 2, 128, 2048], F32, "ExternalInput")
    c.mask_hb = dr("mask", [128, 896], BF16, "ExternalInput")
    c.id_hb = dr("ident", [128, 128], BF16, "ExternalInput")
    c.out_hb = dr("out", [nb, T, D], F32, "ExternalOutput")
    di = lambda name, shape: nc.dram_tensor(name, shape, BF16, kind="Internal").ap()
    c.wqB = di("wqB", [NH, 128, 2048])
    c.wkB = di("wkB", [NH, 128, 2048])
    c.wvB = di("wvB", [NH, 128, 2048])
    c.woB = di("woB", [4, 128, 8192])
    c.wgB = di("wgB", [NHT, 128, 2048])
    c.wuB = di("wuB", [NHT, 128, 2048])
    c.wdB = di("wdB", [8, 4, 2, 128, 2048])

    sb = lambda name, shape, dt: c.es.enter_context(nc.sbuf_tensor(name, shape, dt))
    c.R1 = sb("R1", [128, 33792], BF16)
    c.R2 = sb("R2", [128, 32768], BF16)
    c.R3 = sb("R3", [128, 16384], F32)
    c.ident = sb("identsb", [128, 128], BF16)
    c.ones_st = sb("ones_st", [128, 128], BF16)
    c.ones_row = sb("ones_row", [1, 128], BF16)
    c.bqk = sb("bqksb", [128, 32], F32)
    c.epsv = sb("epsv", [128, 1], F32)
    c.eps4 = sb("eps4", [128, 1], F32)
    c.tick = sb("tick", [1, 1], F32)
    c.st24 = sb("st24", [128, 96], F32)
    c.st2 = sb("st2", [128, 36], F32)
    c.rstd = sb("rstd", [128, 18], F32)
    c.dst = sb("dst", [128, 1024], F32)
    c.wdg2 = sb("wdg2", [128, 4096], BF16)

    c.xT = c.R1[:, 0:32768]
    c.woV = c.R1[:, 0:32768].rearrange("p (a j) -> p a j", a=16)
    c.xn2h = c.R1[:, 0:16384]
    c.wgs = c.R1[:, 16384:18432]
    c.wus = c.R1[:, 18432:20480]
    c.wdg = c.R1[:, 20480:24576]
    c.gated = [c.R1[:, 24576 + s * 1024: 24576 + (s + 1) * 1024] for s in range(8)]
    c.stmp = c.R1[:, 32768:33792]
    c.ATT = c.R2[:, :]
    c.ATTv = c.R2[:, :].rearrange("p (tt j) -> p tt j", tt=16)
    c.ATT4 = c.R2[:, :].rearrange("p (tt jt j) -> p tt jt j", tt=16, jt=16)
    R3b = c.R3[:, :].bitcast(BF16)
    c.R3b = R3b
    c.xland = [c.R3[:, 0:2048], c.R3[:, 2048:4096]]
    c.xntmp = [R3b[:, 8192:10240], R3b[:, 10240:12288]]
    c.ntmp = c.R3[:, 6144:8192]
    c.bo_sb = R3b[0:1, 24576:26624]
    c.wst = [R3b[:, 0:2048], R3b[:, 2048:4096]]
    c.kbuf = [R3b[:, 0:2048], R3b[:, 2048:4096]]
    c.vbuf = [R3b[:, 4096:6144], R3b[:, 6144:8192]]
    c.wkst = [R3b[:, 8192:10240], R3b[:, 10240:12288]]
    c.wvst = [R3b[:, 12288:14336], R3b[:, 14336:16384]]
    c.ebuf = ([R3b[:, 16384 + 512 * i: 16896 + 512 * i] for i in range(4)]
              + [R3b[:, 26624:27136], R3b[:, 27136:27648]])
    c.rbuf = [c.R3[:, 9216:9728], c.R3[:, 9728:10240]]
    c.maskb = R3b[:, 20480:21376]
    c.bv_sb = R3b[0:1, 21504:23552]
    c.xn2tmp = [c.R1[:, 24576:26624], c.R1[:, 26624:28672]]
    c.dacc = c.R3
    c.bounce = [c.wdg2[:, 0:2048], c.wdg2[:, 2048:4096],
                c.dst[:, :].bitcast(BF16)]

    mk = lambda name: Sem(c.es.enter_context(nc.semaphore(name)))
    c.sM = mk("sM")
    c.sMs = mk("sMs")
    c.sMs2 = mk("sMs2")
    c.sXp = [mk("sX0"), mk("sX1")]
    c.sQp = [mk("sQ0"), mk("sQ1")]
    c.sKVp = [mk("sKV0"), mk("sKV1")]
    c.sWO = mk("sWO")
    c.sG = mk("sG")
    c.sU = mk("sU")
    c.sDp = [mk("sD0"), mk("sD1")]
    c.sACp = [mk("sAC0"), mk("sAC1")]
    c.sO = mk("sO")
    c.sCL3 = [mk(f"sCL{j}") for j in range(3)]
    c.sCS3 = [mk(f"sCS{j}") for j in range(3)]
    c.sCR = mk("sCR")
    c.sPE = mk("sPE")
    c.sDV = mk("sDV")
    c.sSC = mk("sSC")


def _emit(c, nb, upto=None, taps=False):
    nc = c.nc
    if taps:
        c.tapR1 = nc.dram_tensor("tapR1", [128, 33792], BF16, kind="ExternalOutput").ap()
        c.tapR2 = nc.dram_tensor("tapR2", [128, 32768], BF16, kind="ExternalOutput").ap()
        c.tapR3 = nc.dram_tensor("tapR3", [128, 16384], F32, kind="ExternalOutput").ap()
    with nc.Block() as blk:
        c.blk = blk

        @blk.gpsimd
        def _(g):
            g.memset(c.ones_st[:], 1.0)
            g.memset(c.ones_row[:], 1.0)
            g.memset(c.epsv[:], EPS)
            g.memset(c.eps4[:], EPS / 4.0)
            if taps:
                g.memset(c.R1[:, :], 0.0)
                g.memset(c.R2[:, :], 0.0)
                g.memset(c.R3[:, :], 0.0)
                g.memset(c.dst[:], 0.0)
                g.memset(c.st24[:], 0.0)
                g.memset(c.st2[:], 0.0)
                g.memset(c.rstd[:], 0.0)

        @blk.sync
        def _(sy):
            inc(sy.dma_start(out=c.ident[:], in_=c.id_hb), c.sM, 16)
            inc(sy.dma_start(out=c.bqk[:], in_=c.bqk_hb), c.sM, 16)

        pcast(c)

        phases = []
        for b in range(nb):
            phases.append((f"p0_{b}", lambda b=b: p0_ln1(c, b)))
            phases.append((f"p1_{b}", lambda b=b: p1_q(c, b)))
            phases.append((f"p2_{b}", lambda b=b: p2_attn(c, b)))
            phases.append((f"p3_{b}", lambda b=b: p3_outproj(c, b)))
            phases.append((f"p4pre_{b}_0", lambda b=b: p4pre_xn2(c, b, 0)))
            phases.append((f"p4_{b}_0", lambda b=b: p4_ffn(c, b, 0)))
            phases.append((f"p4pre_{b}_1", lambda b=b: p4pre_xn2(c, b, 1)))
            phases.append((f"p5_{b}_0", lambda b=b: p5_final(c, b, 0)))
            phases.append((f"p4_{b}_1", lambda b=b: p4_ffn(c, b, 1)))
            phases.append((f"p5_{b}_1", lambda b=b: p5_final(c, b, 1)))
        for name, fn in phases:
            fn()
            if upto is not None and name == upto:
                break

        @blk.sync
        def _(sy):
            if taps:
                for s_ in [c.sMs, c.sWO, c.sG, c.sU] + c.sDp + c.sXp + c.sQp + c.sKVp + c.sACp:
                    sy.wait_ge(s_.h, s_.v)
                w = inc(sy.dma_start(out=c.tapR1, in_=c.R1[:, :]), c.sM, 16)
                w = inc(sy.dma_start(out=c.tapR2, in_=c.R2[:, :]), c.sM, 16)
                w = inc(sy.dma_start(out=c.tapR3, in_=c.R3[:, :]), c.sM, 16)
                sy.wait_ge(c.sM.h, w)
            sy.wait_ge(c.sO.h, c.sO.v)



def pcast(c):
    nc, V = c.nc, c.V
    chunks = []
    for h in range(NH):
        chunks.append((c.wq_hb[h], c.wqB[h], "wq" if h == NH - 1 else None))
    for h in range(NH):
        chunks.append((c.wk_hb[h], c.wkB[h], None))
    for h in range(NH):
        chunks.append((c.wv_hb[h], c.wvB[h], "wv" if h == NH - 1 else None))
    for q in range(4):
        for s in range(4):
            chunks.append((c.wo_hb[q, :, s * 2048:(s + 1) * 2048],
                           c.woB[q, :, s * 2048:(s + 1) * 2048],
                           "wo" if (q, s) == (3, 3) else None))
    for i in range(NHT):
        chunks.append((c.wg_hb[i], c.wgB[i], "wg" if i == NHT - 1 else None))
    for i in range(NHT):
        chunks.append((c.wu_hb[i], c.wuB[i], "wu" if i == NHT - 1 else None))
    for g_ in range(8):
        for dc in range(4):
            for sh in range(2):
                chunks.append((c.wd_hb[g_, dc, sh], c.wdB[g_, dc, sh],
                               "wd" if (g_, dc, sh) == (7, 3, 1) else None))
    n = len(chunks)
    blk = c.blk
    if True:
        @blk.gpsimd
        def _(g):
            V[("p2bv", 0)] = inc(g.dma_start(out=c.bv_sb, in_=c.bv_hb), c.sMs, 16)
            V[("p3bo", 0)] = inc(g.dma_start(out=c.bo_sb, in_=c.bo_hb), c.sMs2, 16)
            for i in range(n + 1):
                if i < n:
                    src, dst, _tag = chunks[i]
                    if i >= 3:
                        g.wait_ge(c.sCS3[i % 3].h, V[("pcS", i - 3)])
                    V[("pcL", i)] = inc(g.dma_start(out=c.bounce[i % 3], in_=src),
                                        c.sCL3[i % 3], 16)
                if i >= 1:
                    _src, dst, tag = chunks[i - 1]
                    g.wait_ge(c.sCL3[(i - 1) % 3].h, V[("pcL", i - 1)])
                    V[("pcS", i - 1)] = inc(g.dma_start(out=dst,
                                                        in_=c.bounce[(i - 1) % 3]),
                                            c.sCS3[(i - 1) % 3], 16)
                    if tag:
                        for j in range(3):
                            g.wait_ge(c.sCS3[j].h, c.sCS3[j].v)
                        V[("cast", tag)] = inc(g.memset(c.tick[:], 0.0), c.sCR)



def ln_stats_emit(c, v, src_ap, pp, pp24=None):
    if pp24 is None:
        pp24 = pp
    o24 = c.st24[:, 24 * pp24:24 * pp24 + 24]
    for i in range(4):
        ins = v.bn_stats(o24[:, 6 * i:6 * i + 6], src_ap[:, 512 * i:512 * (i + 1)])
    w = inc(ins, c.sDV)
    v.wait_ge(c.sDV.h, w)
    ins = v.bn_aggr(c.st2[:, 2 * pp:2 * pp + 2], o24)
    return inc(ins, c.sDV)


def rstd_emit(c, s, pp, wait_dv):
    s.wait_ge(c.sDV.h, wait_dv)
    ins = s.activation(out=c.rstd[:, pp:pp + 1], in_=c.st2[:, 2 * pp + 1:2 * pp + 2],
                       func=AF.Ln, bias=c.epsv[:])
    w = inc(ins, c.sSC)
    s.wait_ge(c.sSC.h, w)
    ins = s.activation(out=c.rstd[:, pp:pp + 1], in_=c.rstd[:, pp:pp + 1],
                       func=AF.Exp, scale=-0.5)
    return inc(ins, c.sSC)



def p0_ln1(c, b):
    nc, V = c.nc, c.V
    with ExitStack() as ps:
        ptr = [ps.enter_context(nc.psum_tensor(f"p0tr{b}_{i}", [128, 2048], BF16)) for i in range(2)]
        blk = c.blk
        if True:
            @blk.sync
            def _(sy):
                sy.wait_ge(c.sO.h, V[("p0outs", b)])
                for tt in range(NT):
                    if tt >= 2:
                        sy.wait_ge(c.sDV.h, V[("p0n", b, tt - 2)])
                    ins = sy.dma_start(out=c.xland[tt % 2][:],
                                       in_=c.x_hb[b, tt * 128:(tt + 1) * 128, :])
                    V[("p0x", b, tt)] = inc(ins, c.sXp[tt % 2], 16)

            @blk.vector
            def _(v):
                def norm(tt):
                    v.wait_ge(c.sSC.h, V[("p0r", b, tt)])
                    if tt >= 2:
                        v.wait_ge(c.sPE.h, V[("p0t", b, tt - 2)])
                    ins = v.tensor_scalar(c.xntmp[tt % 2][:], c.xland[tt % 2][:],
                                          c.st2[:, 2 * (tt % 2):2 * (tt % 2) + 1],
                                          c.rstd[:, (tt % 2):(tt % 2) + 1],
                                          ALU.subtract, ALU.mult)
                    V[("p0n", b, tt)] = inc(ins, c.sDV)
                for tt in range(NT):
                    if tt >= 1:
                        norm(tt - 1)
                    v.wait_ge(c.sXp[tt % 2].h, V[("p0x", b, tt)])
                    if tt >= 2:
                        v.wait_ge(c.sSC.h, V[("p0r", b, tt - 2)])
                    V[("p0a", b, tt)] = ln_stats_emit(c, v, c.xland[tt % 2], tt % 2)
                norm(NT - 1)

            @blk.scalar
            def _(s):
                if b == 0:
                    s.wait_ge(c.sMs.h, 16)
                xTv = c.xT.rearrange("p (a t) -> p a t", a=16)
                def evac(tt):
                    s.wait_ge(c.sPE.h, V[("p0t", b, tt)])
                    for hh in range(2):
                        src = ptr[tt % 2][:, hh * 1024:(hh + 1) * 1024].rearrange(
                            "p (a t) -> p a t", a=8)
                        ins = s.copy(xTv[:, hh * 8:(hh + 1) * 8, tt * 128:(tt + 1) * 128], src)
                    V[("p0e", b, tt)] = inc(ins, c.sSC)
                for tt in range(NT):
                    if tt >= 2:
                        s.wait_ge(c.sDV.h, V[("p0n", b, tt - 2)])
                    V[("p0r", b, tt)] = rstd_emit(c, s, tt % 2, V[("p0a", b, tt)])
                    if tt >= 1:
                        evac(tt - 1)
                evac(NT - 1)

            @blk.tensor
            def _(t):
                if b == 0:
                    t.wait_ge(c.sM.h, 32)
                for tt in range(NT):
                    t.wait_ge(c.sDV.h, V[("p0n", b, tt)])
                    if tt >= 2:
                        t.wait_ge(c.sSC.h, V[("p0e", b, tt - 2)])
                    for dt in range(ND):
                        ins = t.transpose(ptr[tt % 2][:, dt * 128:(dt + 1) * 128],
                                          c.xntmp[tt % 2][:, dt * 128:(dt + 1) * 128],
                                          c.ident[:])
                    V[("p0t", b, tt)] = inc(ins, c.sPE)



def p1_q(c, b):
    nc, V = c.nc, c.V
    with ExitStack() as ps:
        psq = [ps.enter_context(nc.psum_tensor(f"p1q{b}_{i}", [128, 2048], F32)) for i in range(2)]
        blk = c.blk
        if True:
            @blk.sync
            def _(sy):
                sy.wait_ge(c.sCR.h, V[("cast", "wq")])
                sy.wait_ge(c.sO.h, V[("p0outs", b)])
                for jt in range(NH):
                    if jt == 0:
                        sy.wait_ge(c.sDV.h, V[("p0n", b, 14)])
                    if jt == 1:
                        sy.wait_ge(c.sDV.h, V[("p0n", b, 15)])
                    if jt >= 2:
                        sy.wait_ge(c.sPE.h, V[("p1m", b, jt - 2)])
                    ins = sy.dma_start(out=c.wst[jt % 2], in_=c.wqB[jt])
                    V[("p1w", b, jt)] = inc(ins, c.sQp[jt % 2], 16)

            @blk.tensor
            def _(t):
                for jt in range(NH):
                    t.wait_ge(c.sQp[jt % 2].h, V[("p1w", b, jt)])
                    if jt >= 2:
                        t.wait_ge(c.sDV.h, V[("p1e", b, jt - 2)])
                    if jt == 1:
                        t.wait_ge(c.sSC.h, V[("p0e", b, NT - 1)])
                    for ch in range(NCH):
                        if jt == 0:
                            t.wait_ge(c.sSC.h, V[("p0e", b, 4 * ch + 3)])
                        for dt in range(ND):
                            ins = t.matmul(psq[(jt + 1) % 2][:, ch * 512:(ch + 1) * 512],
                                           c.wst[jt % 2][:, dt * 128:(dt + 1) * 128],
                                           c.xT[:, dt * 2048 + ch * 512: dt * 2048 + (ch + 1) * 512],
                                           start=(dt == 0), stop=(dt == ND - 1))
                    V[("p1m", b, jt)] = inc(ins, c.sPE)

            @blk.vector
            def _(v):
                if b == 0:
                    v.wait_ge(c.sM.h, 32)
                for jt in range(NH):
                    v.wait_ge(c.sPE.h, V[("p1m", b, jt)])
                    for ch in range(NCH):
                        ins = v.tensor_scalar(c.ATT4[:, 4 * ch:4 * ch + 4, jt, :],
                                              psq[(jt + 1) % 2][:, ch * 512:(ch + 1) * 512],
                                              c.bqk[:, jt:jt + 1], None, ALU.add)
                    V[("p1e", b, jt)] = inc(ins, c.sDV)



def prev_u(h, ch, u, back):
    for _ in range(back):
        if u > 0:
            u -= 1
        elif ch > 0:
            ch -= 1
            u = 4 * ch + 3
        else:
            h -= 1
            ch = 3
            u = 15
    return h, ch, u


def p2_attn(c, b):
    nc, V = c.nc, c.V
    LOOK = 5
    tiles = [(ch, u) for ch in range(NCH) for u in range(4 * ch + 4)]
    NSI = len(tiles)

    def gprev(h, si, back):
        si -= back
        if si >= 0:
            return h, si
        return h - 1, NSI + si

    with ExitStack() as ps:
        ps4 = [ps.enter_context(nc.psum_tensor(f"p2s{b}_{i}", [128, 512], F32)) for i in range(4)]
        pav = [ps.enter_context(nc.psum_tensor(f"p2av{b}_{i}", [128, 512], F32)) for i in range(2)]
        prs = [ps.enter_context(nc.psum_tensor(f"p2rs{b}_{i}", [128, 512], F32)) for i in range(2)]
        blk = c.blk
        if True:
            @blk.sync
            def _(sy):
                sy.wait_ge(c.sO.h, V[("p0outs", b)])
                V[("p2msk", b)] = inc(sy.dma_start(out=c.maskb[:], in_=c.mask_hb), c.sM, 16)
                sy.wait_ge(c.sCR.h, V[("cast", "wv")])
                for h in range(NH):
                    if h >= 2:
                        sy.wait_ge(c.sPE.h, V[("p2vm", b, h - 2, 3)])
                    ins = sy.dma_start(out=c.wkst[h % 2], in_=c.wkB[h])
                    inc(ins, c.sKVp[h % 2], 16)
                    ins = sy.dma_start(out=c.wvst[h % 2], in_=c.wvB[h])
                    V[("p2wkv", b, h)] = inc(ins, c.sKVp[h % 2], 16)
                sy.wait_ge(c.sPE.h, V[("p2vm", b, NH - 1, 3)])
                sy.wait_ge(c.sCR.h, V[("cast", "wo")])
                for q in range(4):
                    ins = sy.dma_start(out=c.R1[:, q * 8192:(q + 1) * 8192],
                                      in_=c.woB[q])
                    V[("p2wo", b)] = inc(ins, c.sWO, 16)

            @blk.gpsimd
            def _(g):
                if b >= 1:
                    g.wait_ge(c.sO.h, V[("p0outs", b)])
                    V[("p2bv", b)] = inc(g.dma_start(out=c.bv_sb, in_=c.bv_hb), c.sMs, 16)

            @blk.tensor
            def _(t):
                t.wait_ge(c.sMs.h, V[("p2bv", b)])
                t.wait_ge(c.sDV.h, V[("p1e", b, NH - 1)])

                def emit_S(h, si):
                    ch, u = tiles[si]
                    if si < 4:
                        t.wait_ge(c.sDV.h, V[("p2ve", b, h, si)])
                    else:
                        t.wait_ge(c.sSC.h, V[("p2ex", b, h, si - 4)])
                    ins = t.matmul(ps4[si % 4][:],
                                   c.kbuf[h % 2][:, u * 128:(u + 1) * 128],
                                   c.ATT4[:, 4 * ch:4 * ch + 4, h, :],
                                   start=True, stop=True)
                    V[("p2sm", b, h, si)] = inc(ins, c.sPE)

                def emit_AV(h, si):
                    ch, u = tiles[si]
                    nu = 4 * ch + 4
                    if u >= 4 * ch:
                        t.wait_ge(c.sDV.h, V[("p2mk", b, h, si)])
                    else:
                        t.wait_ge(c.sSC.h, V[("p2ex", b, h, si)])
                    if u == 0 and (h, ch) > (0, 1):
                        ph, pc = (h, ch - 2) if ch >= 2 else (h - 1, ch + 2)
                        t.wait_ge(c.sDV.h, V[("p2no", b, ph, pc)])
                        t.wait_ge(c.sSC.h, V[("p2rc", b, ph, pc)])
                    ins = t.matmul(pav[ch % 2][:],
                                   c.vbuf[h % 2][:, u * 128:(u + 1) * 128],
                                   c.ebuf[si % 6][:],
                                   start=(u == 0), stop=(u == nu - 1))
                    V[("p2am", b, h, si)] = inc(ins, c.sPE)
                    ins = t.matmul(prs[ch % 2][:], c.ones_st[:], c.ebuf[si % 6][:],
                                   start=(u == 0), stop=(u == nu - 1))
                    if u == nu - 1:
                        V[("p2rm", b, h, ch)] = inc(ins, c.sPE)

                for h in range(NH):
                    t.wait_ge(c.sKVp[h % 2].h, V[("p2wkv", b, h)])
                    for kc in range(4):
                        if h >= 1:
                            t.wait_ge(c.sSC.h, V[("p2ex", b, h - 1, 36 + kc)])
                        for dt in range(ND):
                            ins = t.matmul(ps4[kc][:],
                                           c.wkst[h % 2][:, dt * 128:(dt + 1) * 128],
                                           c.xT[:, dt * 2048 + kc * 512: dt * 2048 + (kc + 1) * 512],
                                           start=(dt == 0), stop=(dt == ND - 1))
                        V[("p2km", b, h, kc)] = inc(ins, c.sPE)
                    for vb in range(4):
                        t.wait_ge(c.sDV.h, V[("p2ke", b, h, vb)])
                        for q in range(4):
                            vt = vb * 4 + q
                            sl = ps4[vb][:, q * 128:(q + 1) * 128]
                            for dt in range(ND):
                                t.matmul(sl,
                                         c.xT[:, dt * 2048 + vt * 128: dt * 2048 + (vt + 1) * 128],
                                         c.wvst[h % 2][:, dt * 128:(dt + 1) * 128],
                                         start=(dt == 0), stop=False)
                            ins = t.matmul(sl, c.ones_row[:],
                                           c.bv_sb[0:1, h * 128:(h + 1) * 128],
                                           start=False, stop=True)
                        V[("p2vm", b, h, vb)] = inc(ins, c.sPE)
                        if vb >= 1:
                            emit_S(h, vb - 1)
                    t.wait_ge(c.sDV.h, V[("p2ke", b, h, 3)])
                    t.wait_ge(c.sDV.h, V[("p2ve", b, h, 3)])
                    for k in range(3, NSI + LOOK):
                        if k < NSI:
                            emit_S(h, k)
                        if k >= LOOK:
                            emit_AV(h, k - LOOK)

            @blk.scalar
            def _(s):
                s.wait_ge(c.sO.h, V[("p0outs", b)])

                def emit_rc(hh, cc):
                    s.wait_ge(c.sPE.h, V[("p2rm", b, hh, cc)])
                    ins = s.activation(out=c.rbuf[cc % 2][:], in_=prs[cc % 2][:],
                                       func=AF.Ln)
                    w = inc(ins, c.sSC)
                    s.wait_ge(c.sSC.h, w)
                    ins = s.activation(out=c.rbuf[cc % 2][:], in_=c.rbuf[cc % 2][:],
                                       func=AF.Exp, scale=-1.0)
                    V[("p2rc", b, hh, cc)] = inc(ins, c.sSC)

                due = defaultdict(list)
                for h in range(NH):
                    for cc, last in enumerate([3, 11, 23, 39]):
                        due[h * NSI + min(last + 2, NSI - 1)].append((h, cc))
                for g in range(NH * NSI):
                    h, si = divmod(g, NSI)
                    ch, u = tiles[si]
                    s.wait_ge(c.sPE.h, V[("p2sm", b, h, si)])
                    ph, psi = gprev(h, si, 6)
                    if ph >= 0:
                        s.wait_ge(c.sPE.h, V[("p2am", b, ph, psi)])
                    ins = s.activation(out=c.ebuf[si % 6][:], in_=ps4[si % 4][:],
                                       func=AF.Exp, scale=RSQ)
                    V[("p2ex", b, h, si)] = inc(ins, c.sSC)
                    for hh, cc in due.get(g, []):
                        emit_rc(hh, cc)

            @blk.vector
            def _(v):
                v.wait_ge(c.sO.h, V[("p0outs", b)])
                v.wait_ge(c.sM.h, V[("p2msk", b)])
                v.wait_ge(c.sMs.h, V[("p2bv", b)])
                for h in range(NH):
                    for kc in range(4):
                        v.wait_ge(c.sPE.h, V[("p2km", b, h, kc)])
                        ins = v.tensor_scalar(c.kbuf[h % 2][:, kc * 512:(kc + 1) * 512],
                                              ps4[kc][:], c.bqk[:, 16 + h:17 + h],
                                              None, ALU.add)
                        V[("p2ke", b, h, kc)] = inc(ins, c.sDV)
                    for vb in range(4):
                        v.wait_ge(c.sPE.h, V[("p2vm", b, h, vb)])
                        ins = v.tensor_copy(c.vbuf[h % 2][:, vb * 512:(vb + 1) * 512],
                                            ps4[vb][:])
                        V[("p2ve", b, h, vb)] = inc(ins, c.sDV)
                    for si in range(NSI):
                        ch, u = tiles[si]
                        nu = 4 * ch + 4
                        if u >= 4 * ch:
                            dlt = 128 * (u - 4 * ch)
                            v.wait_ge(c.sSC.h, V[("p2ex", b, h, si)])
                            ins = v.tensor_tensor(c.ebuf[si % 6][:], c.ebuf[si % 6][:],
                                                  c.maskb[:, 384 - dlt:896 - dlt], ALU.mult)
                            V[("p2mk", b, h, si)] = inc(ins, c.sDV)
                        if u == nu - 1:
                            v.wait_ge(c.sPE.h, V[("p2am", b, h, si)])
                            v.wait_ge(c.sSC.h, V[("p2rc", b, h, ch)])
                            ins = v.tensor_tensor(c.ATT4[:, 4 * ch:4 * ch + 4, h, :],
                                                  pav[ch % 2][:], c.rbuf[ch % 2][:], ALU.mult)
                            V[("p2no", b, h, ch)] = inc(ins, c.sDV)



def p3_outproj(c, b):
    nc, V = c.nc, c.V
    with ExitStack() as ps:
        po = [ps.enter_context(nc.psum_tensor(f"p3o{b}_{i}", [128, 2048], F32)) for i in range(2)]
        blk = c.blk
        if True:
            @blk.gpsimd
            def _(g):
                if b >= 1:
                    g.wait_ge(c.sO.h, V[("p0outs", b)])
                    V[("p3bo", b)] = inc(g.dma_start(out=c.bo_sb, in_=c.bo_hb), c.sMs2, 16)

            @blk.sync
            def _(sy):
                sy.wait_ge(c.sPE.h, V[("p2am", b, NH - 1, 39)])
                for tt in range(NT):
                    if tt >= 2:
                        sy.wait_ge(c.sDV.h, V[("p3res", b, tt - 2)])
                    ins = sy.dma_start(out=c.xland[tt % 2][:],
                                       in_=c.x_hb[b, tt * 128:(tt + 1) * 128, :])
                    V[("p3x", b, tt)] = inc(ins, c.sXp[tt % 2], 16)

            @blk.tensor
            def _(t):
                t.wait_ge(c.sWO.h, V[("p2wo", b)])
                t.wait_ge(c.sMs2.h, V[("p3bo", b)])
                t.wait_ge(c.sDV.h, V[("p2no", b, NH - 1, NCH - 1)])
                for tt in range(NT):
                    if tt >= 2:
                        t.wait_ge(c.sDV.h, V[("p3ts", b, tt - 2)])
                    for jd in range(16):
                        for jc in range(NCH):
                            t.matmul(po[tt % 2][:, jc * 512:(jc + 1) * 512],
                                     c.ATT[:, tt * 2048 + jd * 128: tt * 2048 + (jd + 1) * 128],
                                     c.woV[:, jd, jc * 512:(jc + 1) * 512],
                                     start=(jd == 0), stop=False)
                    for jc in range(NCH):
                        ins = t.matmul(po[tt % 2][:, jc * 512:(jc + 1) * 512],
                                       c.ones_row[:], c.bo_sb[0:1, jc * 512:(jc + 1) * 512],
                                       start=False, stop=True)
                    V[("p3m", b, tt)] = inc(ins, c.sPE)

            @blk.vector
            def _(v):
                def norm(tt):
                    v.wait_ge(c.sSC.h, V[("p3r", b, tt)])
                    if tt >= 1:
                        v.wait_ge(c.sDV.h, V[("p3res", b, tt - 1)])
                    for jc in range(NCH):
                        ins = v.tensor_scalar(c.ntmp[:, jc * 512:(jc + 1) * 512],
                                              po[tt % 2][:, jc * 512:(jc + 1) * 512],
                                              c.st2[:, 2 * (tt % 2):2 * (tt % 2) + 1],
                                              c.rstd[:, (tt % 2):(tt % 2) + 1],
                                              ALU.subtract, ALU.mult)
                    V[("p3ts", b, tt)] = inc(ins, c.sDV)
                    v.wait_ge(c.sDV.h, V[("p3ts", b, tt)])
                    v.wait_ge(c.sXp[tt % 2].h, V[("p3x", b, tt)])
                    ins = v.tensor_tensor(c.ATTv[:, tt, :], c.ntmp[:], c.xland[tt % 2][:],
                                          ALU.add)
                    V[("p3res", b, tt)] = inc(ins, c.sDV)
                    if tt < 8:
                        V[("p4pa", b, 0, tt)] = ln_stats_emit(
                            c, v, c.ATTv[:, tt, :], 2 + tt, pp24=2 + tt % 2)
                for tt in range(NT):
                    if tt >= 1:
                        norm(tt - 1)
                    v.wait_ge(c.sPE.h, V[("p3m", b, tt)])
                    if tt >= 2:
                        v.wait_ge(c.sSC.h, V[("p3r", b, tt - 2)])
                    V[("p3a", b, tt)] = ln_stats_emit(c, v, po[tt % 2], tt % 2)
                norm(NT - 1)

            @blk.scalar
            def _(s):
                for tt in range(NT):
                    if tt >= 2:
                        s.wait_ge(c.sDV.h, V[("p3ts", b, tt - 2)])
                    V[("p3r", b, tt)] = rstd_emit(c, s, tt % 2, V[("p3a", b, tt)])
                    if tt < 8:
                        V[("p4pr", b, 0, tt)] = rstd_emit(
                            c, s, 2 + tt, V[("p4pa", b, 0, tt)])



def p4pre_xn2(c, b, half):
    nc, V = c.nc, c.V
    stg = c.xntmp if half == 0 else c.xn2tmp
    with ExitStack() as ps:
        ptr = [ps.enter_context(nc.psum_tensor(f"p4p{b}_{half}_{i}", [128, 2048], BF16)) for i in range(2)]
        blk = c.blk
        if True:
            @blk.vector
            def _(v):
                for i in range(8):
                    tt = half * 8 + i
                    v.wait_ge(c.sSC.h, V[("p4pr", b, half, i)])
                    if i >= 2:
                        v.wait_ge(c.sPE.h, V[("p4pt", b, half, i - 2)])
                    pp = 2 + i
                    ins = v.tensor_scalar(stg[i % 2][:], c.ATTv[:, tt, :],
                                          c.st2[:, 2 * pp:2 * pp + 1],
                                          c.rstd[:, pp:pp + 1],
                                          ALU.subtract, ALU.mult)
                    V[("p4pn", b, half, i)] = inc(ins, c.sDV)

            @blk.scalar
            def _(s):
                xnv = c.xn2h.rearrange("p (a t) -> p a t", a=16)
                for i in range(8):
                    s.wait_ge(c.sPE.h, V[("p4pt", b, half, i)])
                    for hh in range(2):
                        src = ptr[i % 2][:, hh * 1024:(hh + 1) * 1024].rearrange(
                            "p (a t) -> p a t", a=8)
                        ins = s.copy(xnv[:, hh * 8:(hh + 1) * 8, i * 128:(i + 1) * 128], src)
                    V[("p4pe", b, half, i)] = inc(ins, c.sSC)

            @blk.tensor
            def _(t):
                for i in range(8):
                    t.wait_ge(c.sDV.h, V[("p4pn", b, half, i)])
                    if i >= 2:
                        t.wait_ge(c.sSC.h, V[("p4pe", b, half, i - 2)])
                    for dt in range(ND):
                        ins = t.transpose(ptr[i % 2][:, dt * 128:(dt + 1) * 128],
                                          stg[i % 2][:, dt * 128:(dt + 1) * 128],
                                          c.ident[:])
                    V[("p4pt", b, half, i)] = inc(ins, c.sPE)



def prev_dtile(grp, dc, tt2, back):
    k = grp * 32 + dc * 8 + tt2 - back
    return k // 32, (k % 32) // 8, k % 8


def p4_ffn(c, b, half):
    nc, V = c.nc, c.V
    wdgb = [c.wdg, c.wdg2[:, :]]
    with ExitStack() as ps:
        pg = [ps.enter_context(nc.psum_tensor(f"p4g{b}_{half}_{i}", [128, 512], F32)) for i in range(2)]
        pu = [ps.enter_context(nc.psum_tensor(f"p4u{b}_{half}_{i}", [128, 512], F32)) for i in range(2)]
        pd = [ps.enter_context(nc.psum_tensor(f"p4d{b}_{half}_{i}", [128, 512], F32)) for i in range(2)]
        blk = c.blk
        if True:
            @blk.sync
            def _(sy):
                sy.wait_ge(c.sCR.h, V[("cast", "wu")])
                if half == 0:
                    sy.wait_ge(c.sPE.h, V[("p3m", b, NT - 1)])
                else:
                    sy.wait_ge(c.sPE.h, V[("p4um", b, 0, NHT - 1, 1)])
                for ht in range(NHT):
                    if ht >= 1:
                        sy.wait_ge(c.sPE.h, V[("p4gm", b, half, ht - 1, 1)])
                    ins = sy.dma_start(out=c.wgs, in_=c.wgB[ht])
                    V[("p4wgv", b, half, ht)] = inc(ins, c.sG, 16)
                    if ht >= 1:
                        sy.wait_ge(c.sPE.h, V[("p4um", b, half, ht - 1, 1)])
                    ins = sy.dma_start(out=c.wus, in_=c.wuB[ht])
                    V[("p4wuv", b, half, ht)] = inc(ins, c.sU, 16)

            @blk.tensor
            def _(t):
                t.wait_ge(c.sSC.h, V[("p4pe", b, half, 7)])

                def gateup(ht):
                    t.wait_ge(c.sG.h, V[("p4wgv", b, half, ht)])
                    if ht >= 1:
                        t.wait_ge(c.sSC.h, V[("p4si", b, half, ht - 1, 0)])
                        t.wait_ge(c.sSC.h, V[("p4si", b, half, ht - 1, 1)])
                    for dt in range(ND):
                        for c2 in range(2):
                            ins = t.matmul(
                                pg[c2][:], c.wgs[:, dt * 128:(dt + 1) * 128],
                                c.xn2h[:, dt * 1024 + c2 * 512: dt * 1024 + (c2 + 1) * 512],
                                start=(dt == 0), stop=(dt == ND - 1))
                    V[("p4gm", b, half, ht, 0)] = V[("p4gm", b, half, ht, 1)] = inc(ins, c.sPE)
                    t.wait_ge(c.sU.h, V[("p4wuv", b, half, ht)])
                    if ht >= 1:
                        t.wait_ge(c.sDV.h, V[("p4ga", b, half, ht - 1, 0)])
                        t.wait_ge(c.sDV.h, V[("p4ga", b, half, ht - 1, 1)])
                    for dt in range(ND):
                        for c2 in range(2):
                            ins = t.matmul(
                                pu[c2][:], c.wus[:, dt * 128:(dt + 1) * 128],
                                c.xn2h[:, dt * 1024 + c2 * 512: dt * 1024 + (c2 + 1) * 512],
                                start=(dt == 0), stop=(dt == ND - 1))
                    V[("p4um", b, half, ht, 0)] = V[("p4um", b, half, ht, 1)] = inc(ins, c.sPE)

                def downblk(grp):
                    t.wait_ge(c.sDV.h, V[("p4ga", b, half, 8 * grp + 7, 1)])
                    for dc in range(4):
                        t.wait_ge(c.sDp[dc % 2].h, V[("p4wd", b, half, grp, dc)])
                        for tt2 in range(8):
                            k = grp * 32 + dc * 8 + tt2
                            if k >= 2:
                                pq, pc2, ptt = prev_dtile(grp, dc, tt2, 2)
                                t.wait_ge(c.sDV.h, V[("p4de", b, half, pq, pc2, ptt)])
                            for sh in range(2):
                                for q in range(4):
                                    sl = sh * 4 + q
                                    ins = t.matmul(
                                        pd[tt2 % 2][:],
                                        c.gated[sl][:, tt2 * 128:(tt2 + 1) * 128],
                                        wdgb[dc % 2][:, sh * 2048 + q * 512:
                                                     sh * 2048 + (q + 1) * 512],
                                        start=(sl == 0), stop=(sl == 7))
                            V[("p4dm", b, half, grp, dc, tt2)] = inc(ins, c.sPE)

                for grp in range(8):
                    for sl in range(8):
                        gateup(8 * grp + sl)
                    downblk(grp)

            @blk.scalar
            def _(s):
                def wd_load2(grp, dc):
                    k = grp * 4 + dc - 2
                    if k >= 0:
                        s.wait_ge(c.sPE.h, V[("p4dm", b, half, k // 4, k % 4, 7)])
                    for sh in range(2):
                        ins = s.dma_start(
                            out=wdgb[dc % 2][:, sh * 2048:(sh + 1) * 2048],
                            in_=c.wdB[grp, dc, sh])
                        V[("p4wd", b, half, grp, dc)] = inc(ins, c.sDp[dc % 2], 16)

                s.wait_ge(c.sCR.h, V[("cast", "wd")])
                if half == 0:
                    s.wait_ge(c.sPE.h, V[("p3m", b, NT - 1)])
                wd_load2(0, 0)
                wd_load2(0, 1)
                for grp in range(8):
                    for sl in range(8):
                        ht = 8 * grp + sl
                        for c2 in range(2):
                            s.wait_ge(c.sPE.h, V[("p4gm", b, half, ht, c2)])
                            if ht >= 1:
                                s.wait_ge(c.sDV.h, V[("p4ga", b, half, ht - 1, c2)])
                            ins = s.activation(out=c.stmp[:, c2 * 512:(c2 + 1) * 512],
                                               in_=pg[c2][:], func=AF.Silu)
                            V[("p4si", b, half, ht, c2)] = inc(ins, c.sSC)
                    for dc in range(4):
                        k = grp * 4 + dc + 2
                        if k < 32:
                            wd_load2(k // 4, k % 4)
                    if half == 0:
                        V[("p4pr", b, 1, grp)] = rstd_emit(
                            c, s, 2 + grp, V[("p4pa", b, 1, grp)])

            @blk.vector
            def _(v):
                def ga(grp, sl):
                    ht = 8 * grp + sl
                    for c2 in range(2):
                        v.wait_ge(c.sSC.h, V[("p4si", b, half, ht, c2)])
                        v.wait_ge(c.sPE.h, V[("p4um", b, half, ht, c2)])
                        if grp >= 1:
                            v.wait_ge(c.sPE.h, V[("p4dm", b, half, grp - 1, 3, 7)])
                        ins = v.tensor_tensor(c.gated[sl][:, c2 * 512:(c2 + 1) * 512],
                                              pu[c2][:], c.stmp[:, c2 * 512:(c2 + 1) * 512],
                                              ALU.mult)
                        V[("p4ga", b, half, ht, c2)] = inc(ins, c.sDV)

                def adds(grp):
                    if grp == 0:
                        v.wait_ge(c.sO.h, V[("p4so", b, half)])
                    for dc in range(4):
                        for tt2 in range(8):
                            v.wait_ge(c.sPE.h, V[("p4dm", b, half, grp, dc, tt2)])
                            dst = c.dacc[:, tt2 * 2048 + dc * 512:
                                         tt2 * 2048 + (dc + 1) * 512]
                            if grp == 0:
                                ins = v.tensor_copy(dst, pd[tt2 % 2][:])
                            else:
                                ins = v.tensor_tensor(dst, pd[tt2 % 2][:], dst, ALU.add)
                            V[("p4de", b, half, grp, dc, tt2)] = inc(ins, c.sDV)
                            if grp == 7 and dc == 3:
                                V[("p5a", b, half, tt2)] = ln_stats_emit(
                                    c, v, c.dacc[:, tt2 * 2048:(tt2 + 1) * 2048],
                                    10 + tt2, pp24=tt2 % 2)

                for grp in range(8):
                    for sl in range(8):
                        ga(grp, sl)
                    adds(grp)
                    if half == 0:
                        V[("p4pa", b, 1, grp)] = ln_stats_emit(
                            c, v, c.ATTv[:, 8 + grp, :], 2 + grp, pp24=2 + grp % 2)



def p5_final(c, b, half):
    nc, V = c.nc, c.V
    blk = c.blk
    if True:
        @blk.vector
        def _(v):
            for tt2 in range(8):
                tt = half * 8 + tt2
                pp = 10 + tt2
                sl = c.dacc[:, tt2 * 2048:(tt2 + 1) * 2048]
                v.wait_ge(c.sSC.h, V[("p5r", b, half, tt2)])
                ins = v.tensor_scalar(sl, sl,
                                      c.st2[:, 2 * pp:2 * pp + 1],
                                      c.rstd[:, pp:pp + 1],
                                      ALU.subtract, ALU.mult)
                w = inc(ins, c.sDV)
                v.wait_ge(c.sDV.h, w)
                ins = v.tensor_tensor(sl, sl, c.ATTv[:, tt, :], ALU.add)
                V[("p5f", b, half, tt2)] = inc(ins, c.sDV)

        @blk.scalar
        def _(s):
            for tt2 in range(8):
                V[("p5r", b, half, tt2)] = rstd_emit(c, s, 10 + tt2, V[("p5a", b, half, tt2)])
            for tt2 in range(8):
                tt = half * 8 + tt2
                s.wait_ge(c.sDV.h, V[("p5f", b, half, tt2)])
                ins = s.dma_start(out=c.out_hb[b, tt * 128:(tt + 1) * 128, :],
                                  in_=c.dacc[:, tt2 * 2048:(tt2 + 1) * 2048])
                inc(ins, c.sO, 16)
    if half == 0:
        V[("p4preo", b, 1)] = c.sO.v
        V[("p4so", b, 1)] = c.sO.v
    else:
        V[("p0outs", b + 1)] = c.sO.v
        V[("p4preo", b + 1, 0)] = c.sO.v
        V[("p4so", b + 1, 0)] = c.sO.v



_CACHE = {}


def _build():
    if "nc" not in _CACHE:
        _CACHE["nc"] = build_nc(NB)
    return _CACHE["nc"]


def _prep_inputs(inputs):
    f = np.ascontiguousarray
    x = np.asarray(inputs["L0_attn_query"])
    def headblock(w):
        return f(np.asarray(w).reshape(16, 128, 16, 128).transpose(2, 1, 0, 3).reshape(
            NH, 128, 2048))
    wq = headblock(inputs["L0_attn_w_q"])
    wk = headblock(inputs["L0_attn_w_k"])
    wv = headblock(inputs["L0_attn_w_v"])
    wo = f(np.asarray(inputs["L0_attn_w_o"]).reshape(4, 4, 128, 2048).transpose(
        0, 2, 1, 3).reshape(4, 128, 8192))
    bqk = np.empty((128, 32), np.float32)
    bqk[:, 0:16] = np.asarray(inputs["L0_attn_b_q"]).reshape(NH, 128).T
    bqk[:, 16:32] = np.asarray(inputs["L0_attn_b_k"]).reshape(NH, 128).T
    bv = f(np.asarray(inputs["L0_attn_b_v"]).reshape(1, D))
    bo = f(np.asarray(inputs["L0_attn_b_o"]).reshape(1, D))
    def hidblock(w):
        return f(np.asarray(w).reshape(64, 128, 16, 128).transpose(0, 3, 2, 1).reshape(
            NHT, 128, 2048))
    wg = hidblock(inputs["L0_ffn_w_gate"])
    wu = hidblock(inputs["L0_ffn_w_up"])
    wd = f(np.asarray(inputs["L0_ffn_w_down"]).T.reshape(8, 2, 4, 128, 4, 512).transpose(
        0, 4, 1, 3, 2, 5).reshape(8, 4, 2, 128, 2048))
    xs = np.arange(896)[None, :]
    pp = np.arange(128)[:, None]
    mask = (xs >= pp + 384).astype(ml_dtypes.bfloat16)
    ident = np.eye(128, dtype=ml_dtypes.bfloat16)
    shared = dict(wq=wq, wk=wk, wv=wv, wo=wo, bqk=bqk, bv=bv, bo=bo,
                  wg=wg, wu=wu, wd=wd, mask=mask, ident=ident)
    maps = []
    for core in range(N_CORES):
        m = dict(shared)
        m["x"] = f(x[core * NB:(core + 1) * NB])
        maps.append(m)
    return maps


def kernel(**inputs):
    nc = _build()
    in_maps = _prep_inputs(inputs)
    res = run_bass_kernel_spmd(nc, in_maps, core_ids=list(range(N_CORES)))
    out = np.concatenate([np.asarray(r["out"]) for r in res.results], axis=0)
    return out


# version 27 (final)
# speedup vs baseline: 1.1976x; 1.1976x over previous
import numpy as np
import ml_dtypes
from contextlib import ExitStack
from collections import defaultdict

import concourse.bass as bass
import concourse.mybir as mybir
from concourse.bass_utils import run_bass_kernel_spmd

F32 = mybir.dt.float32
BF16 = mybir.dt.bfloat16
AF = mybir.ActivationFunctionType
ALU = mybir.AluOpType

N_CORES = 8
NB = 2
T = 2048
D = 2048
NT = 16
ND = 16
NH = 16
NCH = 4
NHT = 64
EPS = 1e-5
RSQ = float(1.0 / np.sqrt(128.0))


class Sem:
    __slots__ = ("h", "v")

    def __init__(self, h):
        self.h = h
        self.v = 0


def inc(instr, s, n=1):
    instr.then_inc(s.h, n)
    s.v += n
    return s.v


class Ctx:
    pass


def build_nc(nb=NB, upto=None, taps=False):
    V = defaultdict(int)
    for pass_i in range(2):
        nc = bass.Bass("TRN2", target_bir_lowering=False)
        c = Ctx()
        c.nc = nc
        c.V = V
        with ExitStack() as es:
            c.es = es
            _declare(c, nb)
            _emit(c, nb, upto, taps)
    return nc


def _declare(c, nb):
    nc = c.nc
    dr = lambda name, shape, dt, kind: nc.dram_tensor(name, shape, dt, kind=kind).ap()
    c.x_hb = dr("x", [nb, T, D], F32, "ExternalInput")
    c.wq_hb = dr("wq", [NH, 128, 2048], F32, "ExternalInput")
    c.wk_hb = dr("wk", [NH, 128, 2048], F32, "ExternalInput")
    c.wv_hb = dr("wv", [NH, 128, 2048], F32, "ExternalInput")
    c.wo_hb = dr("wo", [4, 128, 8192], F32, "ExternalInput")
    c.bqk_hb = dr("bqk", [128, 32], F32, "ExternalInput")
    c.bv_hb = dr("bv", [1, D], F32, "ExternalInput")
    c.bo_hb = dr("bo", [1, D], F32, "ExternalInput")
    c.wg_hb = dr("wg", [NHT, 128, 2048], F32, "ExternalInput")
    c.wu_hb = dr("wu", [NHT, 128, 2048], F32, "ExternalInput")
    c.wd_hb = dr("wd", [8, 4, 2, 128, 2048], F32, "ExternalInput")
    c.mask_hb = dr("mask", [128, 896], BF16, "ExternalInput")
    c.id_hb = dr("ident", [128, 128], BF16, "ExternalInput")
    c.out_hb = dr("out", [nb, T, D], F32, "ExternalOutput")
    di = lambda name, shape: nc.dram_tensor(name, shape, BF16, kind="Internal").ap()
    c.wqB = di("wqB", [NH, 128, 2048])
    c.wkB = di("wkB", [NH, 128, 2048])
    c.wvB = di("wvB", [NH, 128, 2048])
    c.woB = di("woB", [4, 128, 8192])
    c.wgB = di("wgB", [NHT, 128, 2048])
    c.wuB = di("wuB", [NHT, 128, 2048])
    c.wdB = di("wdB", [8, 4, 2, 128, 2048])

    sb = lambda name, shape, dt: c.es.enter_context(nc.sbuf_tensor(name, shape, dt))
    c.R1 = sb("R1", [128, 33792], BF16)
    c.R2 = sb("R2", [128, 32768], BF16)
    c.R3 = sb("R3", [128, 16384], F32)
    c.ident = sb("identsb", [128, 128], BF16)
    c.ones_st = sb("ones_st", [128, 128], BF16)
    c.ones_row = sb("ones_row", [1, 128], BF16)
    c.bqk = sb("bqksb", [128, 32], F32)
    c.epsv = sb("epsv", [128, 1], F32)
    c.eps4 = sb("eps4", [128, 1], F32)
    c.tick = sb("tick", [1, 1], F32)
    c.st24 = sb("st24", [128, 96], F32)
    c.st2 = sb("st2", [128, 36], F32)
    c.rstd = sb("rstd", [128, 18], F32)
    c.dst = sb("dst", [128, 1024], F32)
    c.wdg2 = sb("wdg2", [128, 4096], BF16)

    c.xT = c.R1[:, 0:32768]
    c.woV = c.R1[:, 0:32768].rearrange("p (a j) -> p a j", a=16)
    c.xn2h = c.R1[:, 0:16384]
    c.wgs = c.R1[:, 16384:18432]
    c.wus = c.R1[:, 18432:20480]
    c.wdg = c.R1[:, 20480:24576]
    c.gated = [c.R1[:, 24576 + s * 1024: 24576 + (s + 1) * 1024] for s in range(8)]
    c.stmp = c.R1[:, 32768:33792]
    c.ATT = c.R2[:, :]
    c.ATTv = c.R2[:, :].rearrange("p (tt j) -> p tt j", tt=16)
    c.ATT4 = c.R2[:, :].rearrange("p (tt jt j) -> p tt jt j", tt=16, jt=16)
    R3b = c.R3[:, :].bitcast(BF16)
    c.R3b = R3b
    c.xland = [c.R3[:, 0:2048], c.R3[:, 2048:4096]]
    c.xntmp = [R3b[:, 8192:10240], R3b[:, 10240:12288]]
    c.ntmp = c.R3[:, 6144:8192]
    c.bo_sb = R3b[0:1, 24576:26624]
    c.wst = [R3b[:, 0:2048], R3b[:, 2048:4096]]
    c.kbuf = [R3b[:, 0:2048], R3b[:, 2048:4096]]
    c.vbuf = [R3b[:, 4096:6144], R3b[:, 6144:8192]]
    c.wkst = [R3b[:, 8192:10240], R3b[:, 10240:12288]]
    c.wvst = [R3b[:, 12288:14336], R3b[:, 14336:16384]]
    c.ebuf = ([R3b[:, 16384 + 512 * i: 16896 + 512 * i] for i in range(4)]
              + [R3b[:, 26624:27136], R3b[:, 27136:27648]])
    c.rbuf = [c.R3[:, 9216:9728], c.R3[:, 9728:10240]]
    c.maskb = R3b[:, 20480:21376]
    c.bv_sb = R3b[0:1, 21504:23552]
    c.xn2tmp = [c.R1[:, 24576:26624], c.R1[:, 26624:28672]]
    c.dacc = c.R3
    c.bounce = [c.wdg2[:, 0:2048], c.wdg2[:, 2048:4096],
                c.dst[:, :].bitcast(BF16)]

    mk = lambda name: Sem(c.es.enter_context(nc.semaphore(name)))
    c.sM = mk("sM")
    c.sMs = mk("sMs")
    c.sMs2 = mk("sMs2")
    c.sXp = [mk("sX0"), mk("sX1")]
    c.sQp = [mk("sQ0"), mk("sQ1")]
    c.sKVp = [mk("sKV0"), mk("sKV1")]
    c.sWO = mk("sWO")
    c.sG = mk("sG")
    c.sU = mk("sU")
    c.sDp = [mk("sD0"), mk("sD1")]
    c.sACp = [mk("sAC0"), mk("sAC1")]
    c.sO = mk("sO")
    c.sCL3 = [mk(f"sCL{j}") for j in range(3)]
    c.sCS3 = [mk(f"sCS{j}") for j in range(3)]
    c.sCR = mk("sCR")
    c.sPE = mk("sPE")
    c.sDV = mk("sDV")
    c.sSC = mk("sSC")


def _emit(c, nb, upto=None, taps=False):
    nc = c.nc
    if taps:
        c.tapR1 = nc.dram_tensor("tapR1", [128, 33792], BF16, kind="ExternalOutput").ap()
        c.tapR2 = nc.dram_tensor("tapR2", [128, 32768], BF16, kind="ExternalOutput").ap()
        c.tapR3 = nc.dram_tensor("tapR3", [128, 16384], F32, kind="ExternalOutput").ap()
    with nc.Block() as blk:
        c.blk = blk

        @blk.gpsimd
        def _(g):
            g.memset(c.ones_st[:], 1.0)
            g.memset(c.ones_row[:], 1.0)
            g.memset(c.epsv[:], EPS)
            g.memset(c.eps4[:], EPS / 4.0)
            if taps:
                g.memset(c.R1[:, :], 0.0)
                g.memset(c.R2[:, :], 0.0)
                g.memset(c.R3[:, :], 0.0)
                g.memset(c.dst[:], 0.0)
                g.memset(c.st24[:], 0.0)
                g.memset(c.st2[:], 0.0)
                g.memset(c.rstd[:], 0.0)

        @blk.sync
        def _(sy):
            inc(sy.dma_start(out=c.ident[:], in_=c.id_hb), c.sM, 16)
            inc(sy.dma_start(out=c.bqk[:], in_=c.bqk_hb), c.sM, 16)

        pcast(c)

        phases = []
        for b in range(nb):
            phases.append((f"p0_{b}", lambda b=b: p0_ln1(c, b)))
            phases.append((f"p1_{b}", lambda b=b: p1_q(c, b)))
            phases.append((f"p2_{b}", lambda b=b: p2_attn(c, b)))
            phases.append((f"p3_{b}", lambda b=b: p3_outproj(c, b)))
            phases.append((f"p4pre_{b}_0", lambda b=b: p4pre_xn2(c, b, 0)))
            phases.append((f"p4_{b}_0", lambda b=b: p4_ffn(c, b, 0)))
            phases.append((f"p4pre_{b}_1", lambda b=b: p4pre_xn2(c, b, 1)))
            phases.append((f"p5_{b}_0", lambda b=b: p5_final(c, b, 0)))
            phases.append((f"p4_{b}_1", lambda b=b: p4_ffn(c, b, 1)))
            phases.append((f"p5_{b}_1", lambda b=b: p5_final(c, b, 1)))
        for name, fn in phases:
            fn()
            if upto is not None and name == upto:
                break

        @blk.sync
        def _(sy):
            if taps:
                for s_ in [c.sMs, c.sWO, c.sG, c.sU] + c.sDp + c.sXp + c.sQp + c.sKVp + c.sACp:
                    sy.wait_ge(s_.h, s_.v)
                w = inc(sy.dma_start(out=c.tapR1, in_=c.R1[:, :]), c.sM, 16)
                w = inc(sy.dma_start(out=c.tapR2, in_=c.R2[:, :]), c.sM, 16)
                w = inc(sy.dma_start(out=c.tapR3, in_=c.R3[:, :]), c.sM, 16)
                sy.wait_ge(c.sM.h, w)
            sy.wait_ge(c.sO.h, c.sO.v)



def pcast(c):
    nc, V = c.nc, c.V
    chunks = []
    for h in range(NH):
        chunks.append((c.wq_hb[h], c.wqB[h], "wq" if h == NH - 1 else None))
    for h in range(NH):
        chunks.append((c.wk_hb[h], c.wkB[h], None))
    for h in range(NH):
        chunks.append((c.wv_hb[h], c.wvB[h], "wv" if h == NH - 1 else None))
    for q in range(4):
        for s in range(4):
            chunks.append((c.wo_hb[q, :, s * 2048:(s + 1) * 2048],
                           c.woB[q, :, s * 2048:(s + 1) * 2048],
                           "wo" if (q, s) == (3, 3) else None))
    for i in range(NHT):
        chunks.append((c.wg_hb[i], c.wgB[i], "wg" if i == NHT - 1 else None))
    for i in range(NHT):
        chunks.append((c.wu_hb[i], c.wuB[i], "wu" if i == NHT - 1 else None))
    for g_ in range(8):
        for dc in range(4):
            for sh in range(2):
                chunks.append((c.wd_hb[g_, dc, sh], c.wdB[g_, dc, sh],
                               "wd" if (g_, dc, sh) == (7, 3, 1) else None))
    n = len(chunks)
    blk = c.blk
    if True:
        @blk.gpsimd
        def _(g):
            V[("p2bv", 0)] = inc(g.dma_start(out=c.bv_sb, in_=c.bv_hb), c.sMs, 16)
            V[("p3bo", 0)] = inc(g.dma_start(out=c.bo_sb, in_=c.bo_hb), c.sMs2, 16)
            for i in range(n + 1):
                if i < n:
                    src, dst, _tag = chunks[i]
                    if i >= 3:
                        g.wait_ge(c.sCS3[i % 3].h, V[("pcS", i - 3)])
                    V[("pcL", i)] = inc(g.dma_start(out=c.bounce[i % 3], in_=src),
                                        c.sCL3[i % 3], 16)
                if i >= 1:
                    _src, dst, tag = chunks[i - 1]
                    g.wait_ge(c.sCL3[(i - 1) % 3].h, V[("pcL", i - 1)])
                    V[("pcS", i - 1)] = inc(g.dma_start(out=dst,
                                                        in_=c.bounce[(i - 1) % 3]),
                                            c.sCS3[(i - 1) % 3], 16)
                    if tag:
                        for j in range(3):
                            g.wait_ge(c.sCS3[j].h, c.sCS3[j].v)
                        V[("cast", tag)] = inc(g.memset(c.tick[:], 0.0), c.sCR)



def ln_stats_emit(c, v, src_ap, pp, pp24=None):
    if pp24 is None:
        pp24 = pp
    o24 = c.st24[:, 24 * pp24:24 * pp24 + 24]
    for i in range(4):
        ins = v.bn_stats(o24[:, 6 * i:6 * i + 6], src_ap[:, 512 * i:512 * (i + 1)])
    w = inc(ins, c.sDV)
    v.wait_ge(c.sDV.h, w)
    ins = v.bn_aggr(c.st2[:, 2 * pp:2 * pp + 2], o24)
    return inc(ins, c.sDV)


def rstd_emit(c, s, pp, wait_dv):
    s.wait_ge(c.sDV.h, wait_dv)
    ins = s.activation(out=c.rstd[:, pp:pp + 1], in_=c.st2[:, 2 * pp + 1:2 * pp + 2],
                       func=AF.Ln, bias=c.epsv[:])
    w = inc(ins, c.sSC)
    s.wait_ge(c.sSC.h, w)
    ins = s.activation(out=c.rstd[:, pp:pp + 1], in_=c.rstd[:, pp:pp + 1],
                       func=AF.Exp, scale=-0.5)
    return inc(ins, c.sSC)



def p0_ln1(c, b):
    nc, V = c.nc, c.V
    with ExitStack() as ps:
        ptr = [ps.enter_context(nc.psum_tensor(f"p0tr{b}_{i}", [128, 2048], BF16)) for i in range(2)]
        blk = c.blk
        if True:
            @blk.sync
            def _(sy):
                sy.wait_ge(c.sO.h, V[("p0outs", b)])
                for tt in range(NT):
                    if tt >= 2:
                        sy.wait_ge(c.sDV.h, V[("p0n", b, tt - 2)])
                    ins = sy.dma_start(out=c.xland[tt % 2][:],
                                       in_=c.x_hb[b, tt * 128:(tt + 1) * 128, :])
                    V[("p0x", b, tt)] = inc(ins, c.sXp[tt % 2], 16)

            @blk.vector
            def _(v):
                def norm(tt):
                    v.wait_ge(c.sSC.h, V[("p0r", b, tt)])
                    if tt >= 2:
                        v.wait_ge(c.sPE.h, V[("p0t", b, tt - 2)])
                    ins = v.tensor_scalar(c.xntmp[tt % 2][:], c.xland[tt % 2][:],
                                          c.st2[:, 2 * (tt % 2):2 * (tt % 2) + 1],
                                          c.rstd[:, (tt % 2):(tt % 2) + 1],
                                          ALU.subtract, ALU.mult)
                    V[("p0n", b, tt)] = inc(ins, c.sDV)
                for tt in range(NT):
                    if tt >= 1:
                        norm(tt - 1)
                    v.wait_ge(c.sXp[tt % 2].h, V[("p0x", b, tt)])
                    if tt >= 2:
                        v.wait_ge(c.sSC.h, V[("p0r", b, tt - 2)])
                    V[("p0a", b, tt)] = ln_stats_emit(c, v, c.xland[tt % 2], tt % 2)
                norm(NT - 1)

            @blk.scalar
            def _(s):
                if b == 0:
                    s.wait_ge(c.sMs.h, 16)
                xTv = c.xT.rearrange("p (a t) -> p a t", a=16)
                def evac(tt):
                    s.wait_ge(c.sPE.h, V[("p0t", b, tt)])
                    for hh in range(2):
                        src = ptr[tt % 2][:, hh * 1024:(hh + 1) * 1024].rearrange(
                            "p (a t) -> p a t", a=8)
                        ins = s.copy(xTv[:, hh * 8:(hh + 1) * 8, tt * 128:(tt + 1) * 128], src)
                    V[("p0e", b, tt)] = inc(ins, c.sSC)
                for tt in range(NT):
                    if tt >= 2:
                        s.wait_ge(c.sDV.h, V[("p0n", b, tt - 2)])
                    V[("p0r", b, tt)] = rstd_emit(c, s, tt % 2, V[("p0a", b, tt)])
                    if tt >= 1:
                        evac(tt - 1)
                evac(NT - 1)

            @blk.tensor
            def _(t):
                if b == 0:
                    t.wait_ge(c.sM.h, 32)
                for tt in range(NT):
                    t.wait_ge(c.sDV.h, V[("p0n", b, tt)])
                    if tt >= 2:
                        t.wait_ge(c.sSC.h, V[("p0e", b, tt - 2)])
                    for dt in range(ND):
                        ins = t.transpose(ptr[tt % 2][:, dt * 128:(dt + 1) * 128],
                                          c.xntmp[tt % 2][:, dt * 128:(dt + 1) * 128],
                                          c.ident[:])
                    V[("p0t", b, tt)] = inc(ins, c.sPE)



def p1_q(c, b):
    nc, V = c.nc, c.V
    with ExitStack() as ps:
        psq = [ps.enter_context(nc.psum_tensor(f"p1q{b}_{i}", [128, 2048], F32)) for i in range(2)]
        blk = c.blk
        if True:
            @blk.sync
            def _(sy):
                sy.wait_ge(c.sCR.h, V[("cast", "wq")])
                sy.wait_ge(c.sO.h, V[("p0outs", b)])
                for jt in range(NH):
                    if jt == 0:
                        sy.wait_ge(c.sDV.h, V[("p0n", b, 14)])
                    if jt == 1:
                        sy.wait_ge(c.sDV.h, V[("p0n", b, 15)])
                    if jt >= 2:
                        sy.wait_ge(c.sPE.h, V[("p1m", b, jt - 2)])
                    ins = sy.dma_start(out=c.wst[jt % 2], in_=c.wqB[jt])
                    V[("p1w", b, jt)] = inc(ins, c.sQp[jt % 2], 16)

            @blk.tensor
            def _(t):
                for jt in range(NH):
                    t.wait_ge(c.sQp[jt % 2].h, V[("p1w", b, jt)])
                    if jt >= 2:
                        t.wait_ge(c.sDV.h, V[("p1e", b, jt - 2)])
                    if jt == 1:
                        t.wait_ge(c.sSC.h, V[("p0e", b, NT - 1)])
                    for ch in range(NCH):
                        if jt == 0:
                            t.wait_ge(c.sSC.h, V[("p0e", b, 4 * ch + 3)])
                        for dt in range(ND):
                            ins = t.matmul(psq[(jt + 1) % 2][:, ch * 512:(ch + 1) * 512],
                                           c.wst[jt % 2][:, dt * 128:(dt + 1) * 128],
                                           c.xT[:, dt * 2048 + ch * 512: dt * 2048 + (ch + 1) * 512],
                                           start=(dt == 0), stop=(dt == ND - 1))
                    V[("p1m", b, jt)] = inc(ins, c.sPE)

            @blk.vector
            def _(v):
                if b == 0:
                    v.wait_ge(c.sM.h, 32)
                for jt in range(NH):
                    v.wait_ge(c.sPE.h, V[("p1m", b, jt)])
                    for ch in range(NCH):
                        ins = v.tensor_scalar(c.ATT4[:, 4 * ch:4 * ch + 4, jt, :],
                                              psq[(jt + 1) % 2][:, ch * 512:(ch + 1) * 512],
                                              c.bqk[:, jt:jt + 1], None, ALU.add)
                    V[("p1e", b, jt)] = inc(ins, c.sDV)



def prev_u(h, ch, u, back):
    for _ in range(back):
        if u > 0:
            u -= 1
        elif ch > 0:
            ch -= 1
            u = 4 * ch + 3
        else:
            h -= 1
            ch = 3
            u = 15
    return h, ch, u


def p2_attn(c, b):
    nc, V = c.nc, c.V
    LOOK = 5
    tiles = [(ch, u) for ch in range(NCH) for u in range(4 * ch + 4)]
    NSI = len(tiles)

    def gprev(h, si, back):
        si -= back
        if si >= 0:
            return h, si
        return h - 1, NSI + si

    with ExitStack() as ps:
        ps4 = [ps.enter_context(nc.psum_tensor(f"p2s{b}_{i}", [128, 512], F32)) for i in range(4)]
        pav = [ps.enter_context(nc.psum_tensor(f"p2av{b}_{i}", [128, 512], F32)) for i in range(2)]
        prs = [ps.enter_context(nc.psum_tensor(f"p2rs{b}_{i}", [128, 512], F32)) for i in range(2)]
        blk = c.blk
        if True:
            @blk.sync
            def _(sy):
                sy.wait_ge(c.sO.h, V[("p0outs", b)])
                V[("p2msk", b)] = inc(sy.dma_start(out=c.maskb[:], in_=c.mask_hb), c.sM, 16)
                sy.wait_ge(c.sCR.h, V[("cast", "wv")])
                for h in range(NH):
                    if h >= 2:
                        sy.wait_ge(c.sPE.h, V[("p2vm", b, h - 2, 3)])
                    ins = sy.dma_start(out=c.wkst[h % 2], in_=c.wkB[h])
                    inc(ins, c.sKVp[h % 2], 16)
                    ins = sy.dma_start(out=c.wvst[h % 2], in_=c.wvB[h])
                    V[("p2wkv", b, h)] = inc(ins, c.sKVp[h % 2], 16)
                sy.wait_ge(c.sPE.h, V[("p2vm", b, NH - 1, 3)])
                sy.wait_ge(c.sCR.h, V[("cast", "wo")])
                for q in range(4):
                    ins = sy.dma_start(out=c.R1[:, q * 8192:(q + 1) * 8192],
                                      in_=c.woB[q])
                    V[("p2wo", b)] = inc(ins, c.sWO, 16)

            @blk.gpsimd
            def _(g):
                if b >= 1:
                    g.wait_ge(c.sO.h, V[("p0outs", b)])
                    V[("p2bv", b)] = inc(g.dma_start(out=c.bv_sb, in_=c.bv_hb), c.sMs, 16)

            @blk.tensor
            def _(t):
                t.wait_ge(c.sMs.h, V[("p2bv", b)])
                t.wait_ge(c.sDV.h, V[("p1e", b, NH - 1)])

                def emit_S(h, si):
                    ch, u = tiles[si]
                    if si < 4:
                        t.wait_ge(c.sDV.h, V[("p2ve", b, h, si)])
                    else:
                        t.wait_ge(c.sSC.h, V[("p2ex", b, h, si - 4)])
                    ins = t.matmul(ps4[si % 4][:],
                                   c.kbuf[h % 2][:, u * 128:(u + 1) * 128],
                                   c.ATT4[:, 4 * ch:4 * ch + 4, h, :],
                                   start=True, stop=True)
                    V[("p2sm", b, h, si)] = inc(ins, c.sPE)

                def emit_AV(h, si):
                    ch, u = tiles[si]
                    nu = 4 * ch + 4
                    if u >= 4 * ch:
                        t.wait_ge(c.sDV.h, V[("p2mk", b, h, si)])
                    else:
                        t.wait_ge(c.sSC.h, V[("p2ex", b, h, si)])
                    if u == 0 and (h, ch) > (0, 1):
                        ph, pc = (h, ch - 2) if ch >= 2 else (h - 1, ch + 2)
                        t.wait_ge(c.sDV.h, V[("p2no", b, ph, pc)])
                        t.wait_ge(c.sSC.h, V[("p2rc", b, ph, pc)])
                    ins = t.matmul(pav[ch % 2][:],
                                   c.vbuf[h % 2][:, u * 128:(u + 1) * 128],
                                   c.ebuf[si % 6][:],
                                   start=(u == 0), stop=(u == nu - 1))
                    V[("p2am", b, h, si)] = inc(ins, c.sPE)
                    ins = t.matmul(prs[ch % 2][:], c.ones_st[:], c.ebuf[si % 6][:],
                                   start=(u == 0), stop=(u == nu - 1))
                    if u == nu - 1:
                        V[("p2rm", b, h, ch)] = inc(ins, c.sPE)

                for h in range(NH):
                    t.wait_ge(c.sKVp[h % 2].h, V[("p2wkv", b, h)])
                    for kc in range(4):
                        if h >= 1:
                            t.wait_ge(c.sSC.h, V[("p2ex", b, h - 1, 36 + kc)])
                        for dt in range(ND):
                            ins = t.matmul(ps4[kc][:],
                                           c.wkst[h % 2][:, dt * 128:(dt + 1) * 128],
                                           c.xT[:, dt * 2048 + kc * 512: dt * 2048 + (kc + 1) * 512],
                                           start=(dt == 0), stop=(dt == ND - 1))
                        V[("p2km", b, h, kc)] = inc(ins, c.sPE)
                    for vb in range(4):
                        t.wait_ge(c.sDV.h, V[("p2ke", b, h, vb)])
                        for q in range(4):
                            vt = vb * 4 + q
                            sl = ps4[vb][:, q * 128:(q + 1) * 128]
                            for dt in range(ND):
                                t.matmul(sl,
                                         c.xT[:, dt * 2048 + vt * 128: dt * 2048 + (vt + 1) * 128],
                                         c.wvst[h % 2][:, dt * 128:(dt + 1) * 128],
                                         start=(dt == 0), stop=False)
                            ins = t.matmul(sl, c.ones_row[:],
                                           c.bv_sb[0:1, h * 128:(h + 1) * 128],
                                           start=False, stop=True)
                        V[("p2vm", b, h, vb)] = inc(ins, c.sPE)
                        if vb >= 1:
                            emit_S(h, vb - 1)
                    t.wait_ge(c.sDV.h, V[("p2ke", b, h, 3)])
                    t.wait_ge(c.sDV.h, V[("p2ve", b, h, 3)])
                    for k in range(3, NSI + LOOK):
                        if k < NSI:
                            emit_S(h, k)
                        if k >= LOOK:
                            emit_AV(h, k - LOOK)

            @blk.scalar
            def _(s):
                s.wait_ge(c.sO.h, V[("p0outs", b)])

                def emit_rc(hh, cc):
                    s.wait_ge(c.sPE.h, V[("p2rm", b, hh, cc)])
                    ins = s.activation(out=c.rbuf[cc % 2][:], in_=prs[cc % 2][:],
                                       func=AF.Ln)
                    w = inc(ins, c.sSC)
                    s.wait_ge(c.sSC.h, w)
                    ins = s.activation(out=c.rbuf[cc % 2][:], in_=c.rbuf[cc % 2][:],
                                       func=AF.Exp, scale=-1.0)
                    V[("p2rc", b, hh, cc)] = inc(ins, c.sSC)

                due = defaultdict(list)
                for h in range(NH):
                    for cc, last in enumerate([3, 11, 23, 39]):
                        due[h * NSI + min(last + 2, NSI - 1)].append((h, cc))
                for g in range(NH * NSI):
                    h, si = divmod(g, NSI)
                    ch, u = tiles[si]
                    s.wait_ge(c.sPE.h, V[("p2sm", b, h, si)])
                    ph, psi = gprev(h, si, 6)
                    if ph >= 0:
                        s.wait_ge(c.sPE.h, V[("p2am", b, ph, psi)])
                    ins = s.activation(out=c.ebuf[si % 6][:], in_=ps4[si % 4][:],
                                       func=AF.Exp, scale=RSQ)
                    V[("p2ex", b, h, si)] = inc(ins, c.sSC)
                    for hh, cc in due.get(g, []):
                        emit_rc(hh, cc)

            @blk.vector
            def _(v):
                v.wait_ge(c.sO.h, V[("p0outs", b)])
                v.wait_ge(c.sM.h, V[("p2msk", b)])
                v.wait_ge(c.sMs.h, V[("p2bv", b)])
                for h in range(NH):
                    for kc in range(4):
                        v.wait_ge(c.sPE.h, V[("p2km", b, h, kc)])
                        ins = v.tensor_scalar(c.kbuf[h % 2][:, kc * 512:(kc + 1) * 512],
                                              ps4[kc][:], c.bqk[:, 16 + h:17 + h],
                                              None, ALU.add)
                        V[("p2ke", b, h, kc)] = inc(ins, c.sDV)
                    for vb in range(4):
                        v.wait_ge(c.sPE.h, V[("p2vm", b, h, vb)])
                        ins = v.tensor_copy(c.vbuf[h % 2][:, vb * 512:(vb + 1) * 512],
                                            ps4[vb][:])
                        V[("p2ve", b, h, vb)] = inc(ins, c.sDV)
                    for si in range(NSI):
                        ch, u = tiles[si]
                        nu = 4 * ch + 4
                        if u >= 4 * ch:
                            dlt = 128 * (u - 4 * ch)
                            v.wait_ge(c.sSC.h, V[("p2ex", b, h, si)])
                            ins = v.tensor_tensor(c.ebuf[si % 6][:], c.ebuf[si % 6][:],
                                                  c.maskb[:, 384 - dlt:896 - dlt], ALU.mult)
                            V[("p2mk", b, h, si)] = inc(ins, c.sDV)
                        if u == nu - 1:
                            v.wait_ge(c.sPE.h, V[("p2am", b, h, si)])
                            v.wait_ge(c.sSC.h, V[("p2rc", b, h, ch)])
                            ins = v.tensor_tensor(c.ATT4[:, 4 * ch:4 * ch + 4, h, :],
                                                  pav[ch % 2][:], c.rbuf[ch % 2][:], ALU.mult)
                            V[("p2no", b, h, ch)] = inc(ins, c.sDV)



def p3_outproj(c, b):
    nc, V = c.nc, c.V
    with ExitStack() as ps:
        po = [ps.enter_context(nc.psum_tensor(f"p3o{b}_{i}", [128, 2048], F32)) for i in range(2)]
        blk = c.blk
        if True:
            @blk.gpsimd
            def _(g):
                if b >= 1:
                    g.wait_ge(c.sO.h, V[("p0outs", b)])
                    V[("p3bo", b)] = inc(g.dma_start(out=c.bo_sb, in_=c.bo_hb), c.sMs2, 16)

            @blk.sync
            def _(sy):
                sy.wait_ge(c.sPE.h, V[("p2am", b, NH - 1, 39)])
                for tt in range(NT):
                    if tt >= 2:
                        sy.wait_ge(c.sDV.h, V[("p3res", b, tt - 2)])
                    ins = sy.dma_start(out=c.xland[tt % 2][:],
                                       in_=c.x_hb[b, tt * 128:(tt + 1) * 128, :])
                    V[("p3x", b, tt)] = inc(ins, c.sXp[tt % 2], 16)

            @blk.tensor
            def _(t):
                t.wait_ge(c.sWO.h, V[("p2wo", b)])
                t.wait_ge(c.sMs2.h, V[("p3bo", b)])
                t.wait_ge(c.sDV.h, V[("p2no", b, NH - 1, NCH - 1)])
                for tt in range(NT):
                    if tt >= 2:
                        t.wait_ge(c.sDV.h, V[("p3ts", b, tt - 2)])
                    for jd in range(16):
                        for jc in range(NCH):
                            t.matmul(po[tt % 2][:, jc * 512:(jc + 1) * 512],
                                     c.ATT[:, tt * 2048 + jd * 128: tt * 2048 + (jd + 1) * 128],
                                     c.woV[:, jd, jc * 512:(jc + 1) * 512],
                                     start=(jd == 0), stop=False)
                    for jc in range(NCH):
                        ins = t.matmul(po[tt % 2][:, jc * 512:(jc + 1) * 512],
                                       c.ones_row[:], c.bo_sb[0:1, jc * 512:(jc + 1) * 512],
                                       start=False, stop=True)
                    V[("p3m", b, tt)] = inc(ins, c.sPE)

            @blk.vector
            def _(v):
                def norm(tt):
                    v.wait_ge(c.sSC.h, V[("p3r", b, tt)])
                    if tt >= 1:
                        v.wait_ge(c.sDV.h, V[("p3res", b, tt - 1)])
                    for jc in range(NCH):
                        ins = v.tensor_scalar(c.ntmp[:, jc * 512:(jc + 1) * 512],
                                              po[tt % 2][:, jc * 512:(jc + 1) * 512],
                                              c.st2[:, 2 * (tt % 2):2 * (tt % 2) + 1],
                                              c.rstd[:, (tt % 2):(tt % 2) + 1],
                                              ALU.subtract, ALU.mult)
                    V[("p3ts", b, tt)] = inc(ins, c.sDV)
                    v.wait_ge(c.sDV.h, V[("p3ts", b, tt)])
                    v.wait_ge(c.sXp[tt % 2].h, V[("p3x", b, tt)])
                    ins = v.tensor_tensor(c.ATTv[:, tt, :], c.ntmp[:], c.xland[tt % 2][:],
                                          ALU.add)
                    V[("p3res", b, tt)] = inc(ins, c.sDV)
                    if tt < 8:
                        v.wait_ge(c.sDV.h, V[("p3res", b, tt)])
                        V[("p4pa", b, 0, tt)] = ln_stats_emit(
                            c, v, c.ATTv[:, tt, :], 2 + tt, pp24=2 + tt % 2)
                for tt in range(NT):
                    if tt >= 1:
                        norm(tt - 1)
                    v.wait_ge(c.sPE.h, V[("p3m", b, tt)])
                    if tt >= 2:
                        v.wait_ge(c.sSC.h, V[("p3r", b, tt - 2)])
                    V[("p3a", b, tt)] = ln_stats_emit(c, v, po[tt % 2], tt % 2)
                norm(NT - 1)

            @blk.scalar
            def _(s):
                for tt in range(NT):
                    if tt >= 2:
                        s.wait_ge(c.sDV.h, V[("p3ts", b, tt - 2)])
                    V[("p3r", b, tt)] = rstd_emit(c, s, tt % 2, V[("p3a", b, tt)])
                    if tt < 8:
                        V[("p4pr", b, 0, tt)] = rstd_emit(
                            c, s, 2 + tt, V[("p4pa", b, 0, tt)])



def p4pre_xn2(c, b, half):
    nc, V = c.nc, c.V
    stg = c.xntmp if half == 0 else c.xn2tmp
    with ExitStack() as ps:
        ptr = [ps.enter_context(nc.psum_tensor(f"p4p{b}_{half}_{i}", [128, 2048], BF16)) for i in range(2)]
        blk = c.blk
        if True:
            @blk.vector
            def _(v):
                for i in range(8):
                    tt = half * 8 + i
                    v.wait_ge(c.sSC.h, V[("p4pr", b, half, i)])
                    if i >= 2:
                        v.wait_ge(c.sPE.h, V[("p4pt", b, half, i - 2)])
                    pp = 2 + i
                    ins = v.tensor_scalar(stg[i % 2][:], c.ATTv[:, tt, :],
                                          c.st2[:, 2 * pp:2 * pp + 1],
                                          c.rstd[:, pp:pp + 1],
                                          ALU.subtract, ALU.mult)
                    V[("p4pn", b, half, i)] = inc(ins, c.sDV)

            @blk.scalar
            def _(s):
                xnv = c.xn2h.rearrange("p (a t) -> p a t", a=16)
                for i in range(8):
                    s.wait_ge(c.sPE.h, V[("p4pt", b, half, i)])
                    for hh in range(2):
                        src = ptr[i % 2][:, hh * 1024:(hh + 1) * 1024].rearrange(
                            "p (a t) -> p a t", a=8)
                        ins = s.copy(xnv[:, hh * 8:(hh + 1) * 8, i * 128:(i + 1) * 128], src)
                    V[("p4pe", b, half, i)] = inc(ins, c.sSC)

            @blk.tensor
            def _(t):
                for i in range(8):
                    t.wait_ge(c.sDV.h, V[("p4pn", b, half, i)])
                    if i >= 2:
                        t.wait_ge(c.sSC.h, V[("p4pe", b, half, i - 2)])
                    for dt in range(ND):
                        ins = t.transpose(ptr[i % 2][:, dt * 128:(dt + 1) * 128],
                                          stg[i % 2][:, dt * 128:(dt + 1) * 128],
                                          c.ident[:])
                    V[("p4pt", b, half, i)] = inc(ins, c.sPE)



def prev_dtile(grp, dc, tt2, back):
    k = grp * 32 + dc * 8 + tt2 - back
    return k // 32, (k % 32) // 8, k % 8


def p4_ffn(c, b, half):
    nc, V = c.nc, c.V
    wdgb = [c.wdg, c.wdg2[:, :]]
    with ExitStack() as ps:
        pg = [ps.enter_context(nc.psum_tensor(f"p4g{b}_{half}_{i}", [128, 512], F32)) for i in range(2)]
        pu = [ps.enter_context(nc.psum_tensor(f"p4u{b}_{half}_{i}", [128, 512], F32)) for i in range(2)]
        pd = [ps.enter_context(nc.psum_tensor(f"p4d{b}_{half}_{i}", [128, 512], F32)) for i in range(2)]
        blk = c.blk
        if True:
            @blk.sync
            def _(sy):
                sy.wait_ge(c.sCR.h, V[("cast", "wu")])
                if half == 0:
                    sy.wait_ge(c.sPE.h, V[("p3m", b, NT - 1)])
                else:
                    sy.wait_ge(c.sPE.h, V[("p4um", b, 0, NHT - 1, 1)])
                for ht in range(NHT):
                    if ht >= 1:
                        sy.wait_ge(c.sPE.h, V[("p4gm", b, half, ht - 1, 1)])
                    ins = sy.dma_start(out=c.wgs, in_=c.wgB[ht])
                    V[("p4wgv", b, half, ht)] = inc(ins, c.sG, 16)
                    if ht >= 1:
                        sy.wait_ge(c.sPE.h, V[("p4um", b, half, ht - 1, 1)])
                    ins = sy.dma_start(out=c.wus, in_=c.wuB[ht])
                    V[("p4wuv", b, half, ht)] = inc(ins, c.sU, 16)

            @blk.tensor
            def _(t):
                t.wait_ge(c.sSC.h, V[("p4pe", b, half, 7)])

                def gateup(ht):
                    t.wait_ge(c.sG.h, V[("p4wgv", b, half, ht)])
                    if ht >= 1:
                        t.wait_ge(c.sSC.h, V[("p4si", b, half, ht - 1, 0)])
                        t.wait_ge(c.sSC.h, V[("p4si", b, half, ht - 1, 1)])
                    for dt in range(ND):
                        for c2 in range(2):
                            ins = t.matmul(
                                pg[c2][:], c.wgs[:, dt * 128:(dt + 1) * 128],
                                c.xn2h[:, dt * 1024 + c2 * 512: dt * 1024 + (c2 + 1) * 512],
                                start=(dt == 0), stop=(dt == ND - 1))
                    V[("p4gm", b, half, ht, 0)] = V[("p4gm", b, half, ht, 1)] = inc(ins, c.sPE)
                    t.wait_ge(c.sU.h, V[("p4wuv", b, half, ht)])
                    if ht >= 1:
                        t.wait_ge(c.sDV.h, V[("p4ga", b, half, ht - 1, 0)])
                        t.wait_ge(c.sDV.h, V[("p4ga", b, half, ht - 1, 1)])
                    for dt in range(ND):
                        for c2 in range(2):
                            ins = t.matmul(
                                pu[c2][:], c.wus[:, dt * 128:(dt + 1) * 128],
                                c.xn2h[:, dt * 1024 + c2 * 512: dt * 1024 + (c2 + 1) * 512],
                                start=(dt == 0), stop=(dt == ND - 1))
                    V[("p4um", b, half, ht, 0)] = V[("p4um", b, half, ht, 1)] = inc(ins, c.sPE)

                def downblk(grp):
                    t.wait_ge(c.sDV.h, V[("p4ga", b, half, 8 * grp + 7, 1)])
                    for dc in range(4):
                        t.wait_ge(c.sDp[dc % 2].h, V[("p4wd", b, half, grp, dc)])
                        for tt2 in range(8):
                            k = grp * 32 + dc * 8 + tt2
                            if k >= 2:
                                pq, pc2, ptt = prev_dtile(grp, dc, tt2, 2)
                                t.wait_ge(c.sDV.h, V[("p4de", b, half, pq, pc2, ptt)])
                            for sh in range(2):
                                for q in range(4):
                                    sl = sh * 4 + q
                                    ins = t.matmul(
                                        pd[tt2 % 2][:],
                                        c.gated[sl][:, tt2 * 128:(tt2 + 1) * 128],
                                        wdgb[dc % 2][:, sh * 2048 + q * 512:
                                                     sh * 2048 + (q + 1) * 512],
                                        start=(sl == 0), stop=(sl == 7))
                            V[("p4dm", b, half, grp, dc, tt2)] = inc(ins, c.sPE)

                for grp in range(8):
                    for sl in range(8):
                        gateup(8 * grp + sl)
                    downblk(grp)

            @blk.scalar
            def _(s):
                def wd_load2(grp, dc):
                    k = grp * 4 + dc - 2
                    if k >= 0:
                        s.wait_ge(c.sPE.h, V[("p4dm", b, half, k // 4, k % 4, 7)])
                    for sh in range(2):
                        ins = s.dma_start(
                            out=wdgb[dc % 2][:, sh * 2048:(sh + 1) * 2048],
                            in_=c.wdB[grp, dc, sh])
                        V[("p4wd", b, half, grp, dc)] = inc(ins, c.sDp[dc % 2], 16)

                s.wait_ge(c.sCR.h, V[("cast", "wd")])
                if half == 0:
                    s.wait_ge(c.sPE.h, V[("p3m", b, NT - 1)])
                wd_load2(0, 0)
                wd_load2(0, 1)
                for grp in range(8):
                    for sl in range(8):
                        ht = 8 * grp + sl
                        for c2 in range(2):
                            s.wait_ge(c.sPE.h, V[("p4gm", b, half, ht, c2)])
                            if ht >= 1:
                                s.wait_ge(c.sDV.h, V[("p4ga", b, half, ht - 1, c2)])
                            ins = s.activation(out=c.stmp[:, c2 * 512:(c2 + 1) * 512],
                                               in_=pg[c2][:], func=AF.Silu)
                            V[("p4si", b, half, ht, c2)] = inc(ins, c.sSC)
                    for dc in range(4):
                        k = grp * 4 + dc + 2
                        if k < 32:
                            wd_load2(k // 4, k % 4)
                    if half == 0:
                        V[("p4pr", b, 1, grp)] = rstd_emit(
                            c, s, 2 + grp, V[("p4pa", b, 1, grp)])

            @blk.vector
            def _(v):
                def ga(grp, sl):
                    ht = 8 * grp + sl
                    for c2 in range(2):
                        v.wait_ge(c.sSC.h, V[("p4si", b, half, ht, c2)])
                        v.wait_ge(c.sPE.h, V[("p4um", b, half, ht, c2)])
                        if grp >= 1:
                            v.wait_ge(c.sPE.h, V[("p4dm", b, half, grp - 1, 3, 7)])
                        ins = v.tensor_tensor(c.gated[sl][:, c2 * 512:(c2 + 1) * 512],
                                              pu[c2][:], c.stmp[:, c2 * 512:(c2 + 1) * 512],
                                              ALU.mult)
                        V[("p4ga", b, half, ht, c2)] = inc(ins, c.sDV)

                def adds(grp):
                    if grp == 0:
                        v.wait_ge(c.sO.h, V[("p4so", b, half)])
                    for dc in range(4):
                        for tt2 in range(8):
                            v.wait_ge(c.sPE.h, V[("p4dm", b, half, grp, dc, tt2)])
                            dst = c.dacc[:, tt2 * 2048 + dc * 512:
                                         tt2 * 2048 + (dc + 1) * 512]
                            if grp == 0:
                                ins = v.tensor_copy(dst, pd[tt2 % 2][:])
                            else:
                                ins = v.tensor_tensor(dst, pd[tt2 % 2][:], dst, ALU.add)
                            V[("p4de", b, half, grp, dc, tt2)] = inc(ins, c.sDV)
                            if grp == 7 and dc == 3:
                                v.wait_ge(c.sDV.h,
                                          V[("p4de", b, half, grp, dc, tt2)])
                                V[("p5a", b, half, tt2)] = ln_stats_emit(
                                    c, v, c.dacc[:, tt2 * 2048:(tt2 + 1) * 2048],
                                    10 + tt2, pp24=tt2 % 2)

                for grp in range(8):
                    for sl in range(8):
                        ga(grp, sl)
                    adds(grp)
                    if half == 0:
                        V[("p4pa", b, 1, grp)] = ln_stats_emit(
                            c, v, c.ATTv[:, 8 + grp, :], 2 + grp, pp24=2 + grp % 2)



def p5_final(c, b, half):
    nc, V = c.nc, c.V
    blk = c.blk
    if True:
        @blk.vector
        def _(v):
            for tt2 in range(8):
                tt = half * 8 + tt2
                pp = 10 + tt2
                sl = c.dacc[:, tt2 * 2048:(tt2 + 1) * 2048]
                v.wait_ge(c.sSC.h, V[("p5r", b, half, tt2)])
                ins = v.tensor_scalar(sl, sl,
                                      c.st2[:, 2 * pp:2 * pp + 1],
                                      c.rstd[:, pp:pp + 1],
                                      ALU.subtract, ALU.mult)
                w = inc(ins, c.sDV)
                v.wait_ge(c.sDV.h, w)
                ins = v.tensor_tensor(sl, sl, c.ATTv[:, tt, :], ALU.add)
                V[("p5f", b, half, tt2)] = inc(ins, c.sDV)

        @blk.scalar
        def _(s):
            for tt2 in range(8):
                V[("p5r", b, half, tt2)] = rstd_emit(c, s, 10 + tt2, V[("p5a", b, half, tt2)])
            for tt2 in range(8):
                tt = half * 8 + tt2
                s.wait_ge(c.sDV.h, V[("p5f", b, half, tt2)])
                ins = s.dma_start(out=c.out_hb[b, tt * 128:(tt + 1) * 128, :],
                                  in_=c.dacc[:, tt2 * 2048:(tt2 + 1) * 2048])
                inc(ins, c.sO, 16)
    if half == 0:
        V[("p4preo", b, 1)] = c.sO.v
        V[("p4so", b, 1)] = c.sO.v
    else:
        V[("p0outs", b + 1)] = c.sO.v
        V[("p4preo", b + 1, 0)] = c.sO.v
        V[("p4so", b + 1, 0)] = c.sO.v



_CACHE = {}


def _build():
    if "nc" not in _CACHE:
        _CACHE["nc"] = build_nc(NB)
    return _CACHE["nc"]


def _prep_inputs(inputs):
    f = np.ascontiguousarray
    x = np.asarray(inputs["L0_attn_query"])
    def headblock(w):
        return f(np.asarray(w).reshape(16, 128, 16, 128).transpose(2, 1, 0, 3).reshape(
            NH, 128, 2048))
    wq = headblock(inputs["L0_attn_w_q"])
    wk = headblock(inputs["L0_attn_w_k"])
    wv = headblock(inputs["L0_attn_w_v"])
    wo = f(np.asarray(inputs["L0_attn_w_o"]).reshape(4, 4, 128, 2048).transpose(
        0, 2, 1, 3).reshape(4, 128, 8192))
    bqk = np.empty((128, 32), np.float32)
    bqk[:, 0:16] = np.asarray(inputs["L0_attn_b_q"]).reshape(NH, 128).T
    bqk[:, 16:32] = np.asarray(inputs["L0_attn_b_k"]).reshape(NH, 128).T
    bv = f(np.asarray(inputs["L0_attn_b_v"]).reshape(1, D))
    bo = f(np.asarray(inputs["L0_attn_b_o"]).reshape(1, D))
    def hidblock(w):
        return f(np.asarray(w).reshape(64, 128, 16, 128).transpose(0, 3, 2, 1).reshape(
            NHT, 128, 2048))
    wg = hidblock(inputs["L0_ffn_w_gate"])
    wu = hidblock(inputs["L0_ffn_w_up"])
    wd = f(np.asarray(inputs["L0_ffn_w_down"]).T.reshape(8, 2, 4, 128, 4, 512).transpose(
        0, 4, 1, 3, 2, 5).reshape(8, 4, 2, 128, 2048))
    xs = np.arange(896)[None, :]
    pp = np.arange(128)[:, None]
    mask = (xs >= pp + 384).astype(ml_dtypes.bfloat16)
    ident = np.eye(128, dtype=ml_dtypes.bfloat16)
    shared = dict(wq=wq, wk=wk, wv=wv, wo=wo, bqk=bqk, bv=bv, bo=bo,
                  wg=wg, wu=wu, wd=wd, mask=mask, ident=ident)
    maps = []
    for core in range(N_CORES):
        m = dict(shared)
        m["x"] = f(x[core * NB:(core + 1) * NB])
        maps.append(m)
    return maps


def kernel(**inputs):
    nc = _build()
    in_maps = _prep_inputs(inputs)
    res = run_bass_kernel_spmd(nc, in_maps, core_ids=list(range(N_CORES)))
    out = np.concatenate([np.asarray(r["out"]) for r in res.results], axis=0)
    return out
